# version 20
# baseline (speedup 1.0000x reference)
import numpy as np

import concourse.bass as bass
import concourse.mybir as mybir
from concourse.bass_utils import run_bass_kernel_spmd

F32 = mybir.dt.float32
BF16 = mybir.dt.bfloat16
I32 = mybir.dt.int32
AF = mybir.ActivationFunctionType
ALU = mybir.AluOpType
AX = mybir.AxisListType

D = 1024
HID = 3072
TOK = 2048
NCH = 32
NOWN = 16
DI = 8
DO = 8
NHO = 24
NP = 8
NPL = 4
KD = 512
NS = 4
SL = 512
EPS_LN = 1e-5
EPS_Z = 1e-6
QSCALE = 1.0 / 127.0


def _off():
    o = {}
    o["XNH"] = 0
    o["XNL"] = 32768
    o["WKH"] = 65536
    o["WKL"] = 73728
    o["WVH"] = 81920
    o["WVL"] = 90112
    o["GFM"] = 65536
    o["ATL"] = 131072
    o["KSE"] = 135168
    o["IDT"] = 135680
    o["BPR"] = 136192
    base = 140288
    o["WRAW"] = base
    o["XRAW"] = base + 8192
    o["XNF"] = base + 16384
    o["XNB"] = base + 24576
    o["KEB"] = base + 32768
    o["KRB"] = base + 36864
    o["KTM"] = base + 40960
    o["VKV"] = base + 49152
    o["ST"] = base + 57408
    o["WGU"] = base
    o["WUT_unused"] = base + 4096
    o["WDT"] = base + 12288
    o["GSIL"] = base + 18432
    o["HFB"] = base + 20480
    o["HHI"] = base + 22528
    o["HLO"] = base + 47104
    o["QSL"] = base
    o["ATNH"] = base + 16384
    o["ATNL"] = base + 24576
    o["WQT"] = base + 32768
    o["WOT"] = base + 40960
    o["WR3"] = base + 49152
    o["RZB"] = base + 53248
    o["RZS"] = base + 57344
    o["AF"] = base + 59392
    o["XTC"] = base + 61440
    o["YST"] = base + 65536
    o["KVS"] = base + 53248
    o["CCB"] = base + 61440
    o["SCR"] = base + 69632
    return o


def build_nc(debug=False):
    nc = bass.Bass("TRN2", target_bir_lowering=False)

    x_ord = nc.dram_tensor("x_ord", [4096, D], F32, kind="ExternalInput").ap()
    x_t = nc.dram_tensor("x_t", [D, TOK], F32, kind="ExternalInput").ap()
    wqt = nc.dram_tensor("wqt", [D, D], F32, kind="ExternalInput").ap()
    wkt = nc.dram_tensor("wkt", [D, KD], F32, kind="ExternalInput").ap()
    wvt = nc.dram_tensor("wvt", [D, KD], F32, kind="ExternalInput").ap()
    cc_in = nc.dram_tensor("cc_in", [128, NPL * 129], F32).ap()
    cc_out = nc.dram_tensor("cc_out", [2, 128, NPL * 129], F32).ap()
    wot = nc.dram_tensor("wot", [D, D], F32, kind="ExternalInput").ap()
    wgt_d = nc.dram_tensor("wgt", [D, HID], I32, kind="ExternalInput").ap()
    wut_d = nc.dram_tensor("wut", [D, HID], I32, kind="ExternalInput").ap()
    wdt_d = nc.dram_tensor("wdt", [HID, D], I32, kind="ExternalInput").ap()
    ident = nc.dram_tensor("ident", [128, 128], F32, kind="ExternalInput").ap()
    bpair = nc.dram_tensor("bpair", [16, NP * 128], F32, kind="ExternalInput").ap()
    y = nc.dram_tensor("y", [D, TOK], F32, kind="ExternalOutput").ap()

    dbg = {}
    if debug:
        for name, shape, dt in [
            ("dbg_xnh", [128, DI * TOK], BF16), ("dbg_xnl", [128, DI * TOK], BF16),
            ("dbg_atl", [128, NP * 128], F32), ("dbg_kse", [128, NP * 16], F32),
            ("dbg_g", [128, DO * TOK], F32), ("dbg_q", [128, NP * SL], F32),
            ("dbg_ah", [128, NP * SL], BF16), ("dbg_al", [128, NP * SL], BF16),
            ("dbg_ktm", [128, KD], F32), ("dbg_vkv", [128, NPL * 129], F32),
        ]:
            dbg[name] = nc.dram_tensor(name, shape, dt, kind="ExternalOutput").ap()

    o = _off()
    ARENA = 207 * 1024

    from contextlib import ExitStack
    with ExitStack() as ctx:
        _arena = ctx.enter_context(nc.sbuf_tensor("arena", [128, ARENA], mybir.dt.uint8))
        ps_tp_h = ctx.enter_context(nc.psum_tensor("ps_tp", [128, 1024], F32))
        ps_k_h = ctx.enter_context(nc.psum_tensor("ps_k", [128, 1024], F32))
        ps_v_h = ctx.enter_context(nc.psum_tensor("ps_v", [128, 512], F32))
        ps_kv_h = ctx.enter_context(nc.psum_tensor("ps_kv", [128, 1536], F32))
        sems = {}
        for sname in ["Sd", "Sck0", "Sck1", "Sx0", "Sx1", "Sw0", "Sw1", "Sgg0", "Sgg1", "Sgu0",
                      "Sgu1", "Sgd0", "Sgd1", "Sxt0", "Sxt1", "Sy0", "Sy1", "Sc",
                      "V_st", "V_xn", "V_lo", "V_wp", "V_kt", "V_va", "V_asm",
                      "V_hf", "V_hl", "V_wq", "V_wo", "V_q", "V_rz", "V_af",
                      "V_at", "V_y",
                      "A_sq", "A_rs", "A_hi", "A_ke", "A_gs", "A_hh", "A_g", "A_qe",
                      "A_rzs", "A_ath",
                      "T_tp", "T_k", "T_v", "T_kv", "T_ga", "T_up", "T_dnh",
                      "T_q", "T_z", "T_rb", "T_at", "T_s", "G_kt",
                      "V_stg", "Scc_in", "Scc", "CCs", "Sgg2", "Sgu2",
                      "Swv0", "Swv1", "Swu0", "Swu1", "Swu2"]:
            sems[sname] = ctx.enter_context(nc.semaphore(sname))
        g = sems
        Sx = (g["Sx0"], g["Sx1"]); Sw = (g["Sw0"], g["Sw1"])
        Sgd = (g["Sgd0"], g["Sgd1"]); Sxt = (g["Sxt0"], g["Sxt1"])
        Sy = (g["Sy0"], g["Sy1"]); Sc = g["Sc"]; Sd = g["Sd"]
        Sck = (g["Sck0"], g["Sck1"]); G_kt = g["G_kt"]
        Sgg = (g["Sgg0"], g["Sgg1"], g["Sgg2"])
        Sgu = (g["Sgu0"], g["Sgu1"], g["Sgu2"])
        Swv = (g["Swv0"], g["Swv1"])
        Swu = (g["Swu0"], g["Swu1"], g["Swu2"])
        V_stg = g["V_stg"]; Scc_in = g["Scc_in"]; Scc = g["Scc"]; CCs = g["CCs"]
        A_sq = g["A_sq"]
        (V_st, V_xn, V_lo, V_wp, V_kt, V_va, V_asm,
         V_hf, V_hl, V_wq, V_wo, V_q, V_rz, V_af, V_at, V_y,
         A_rs, A_hi, A_ke, A_gs, A_hh, A_g, A_qe, A_rzs, A_ath,
         T_tp, T_k, T_v, T_kv, T_ga, T_up, T_dnh, T_q, T_z, T_rb, T_at,
         T_s) = (sems[n] for n in [
                 "V_st", "V_xn", "V_lo", "V_wp", "V_kt", "V_va", "V_asm",
                 "V_hf", "V_hl", "V_wq", "V_wo", "V_q", "V_rz", "V_af",
                 "V_at", "V_y",
                 "A_rs", "A_hi", "A_ke", "A_gs", "A_hh", "A_g", "A_qe",
                 "A_rzs", "A_ath",
                 "T_tp", "T_k", "T_v", "T_kv", "T_ga", "T_up", "T_dnh",
                 "T_q", "T_z", "T_rb", "T_at", "T_s"])
        block = ctx.enter_context(nc.Block())
        slab = nc.lookup_mloc(_arena).addr

        def sb(name, shape, dt):
            return nc.alloc_sbuf_tensor_at(name, shape, dt, offset=slab + o[name]).ap()

        XNH = sb("XNH", [128, DI, TOK], BF16)
        XNL = sb("XNL", [128, DI, TOK], BF16)
        WKH = sb("WKH", [128, DI, KD], BF16)
        WKL = sb("WKL", [128, DI, KD], BF16)
        WVH = sb("WVH", [128, DI, KD], BF16)
        WVL = sb("WVL", [128, DI, KD], BF16)
        GFM = sb("GFM", [128, DO, TOK], F32)
        ATL = sb("ATL", [128, NP, 128], F32)
        KSE = sb("KSE", [128, NP, 16], F32)
        IDT = sb("IDT", [128, 128], F32)
        BPR = sb("BPR", [16, NP, 128], F32)
        WRAW = sb("WRAW", [128, 2, D], F32)
        XRAW = sb("XRAW", [128, 2, D], F32)
        XNF = sb("XNF", [128, 2, D], F32)
        XNB = sb("XNB", [128, 2, 2, DI, 128], BF16)
        KEB = sb("KEB", [128, KD], F32)
        KRB = sb("KRB", [128, KD], F32)
        KTM = sb("KTM", [128, 2, KD], F32)
        VKV = sb("VKV", [128, 2, NPL, 129], F32)
        KVS = sb("KVS", [128, NPL, 129], F32)
        CCB = sb("CCB", [128, 2, NPL, 129], F32)
        SCR = sb("SCR", [128, SL], F32)
        ST = sb("ST", [128, 2, 8], F32)
        WGU = sb("WGU", [128, 3, 2, DI, 128], BF16)
        WDT = sb("WDT", [128, 2, 12, 128], BF16)
        GSIL = sb("GSIL", [128, SL], F32)
        HFB = sb("HFB", [128, SL], F32)
        HHI = sb("HHI", [128, NHO, SL], BF16)
        HLO = sb("HLO", [128, NHO, SL], BF16)
        QSL = sb("QSL", [128, NP, SL], F32)
        ATNH = sb("ATNH", [128, NP, SL], BF16)
        ATNL = sb("ATNL", [128, NP, SL], BF16)
        WQT = sb("WQT", [128, 2, 2, DI, 128], BF16)
        WOT = sb("WOT", [128, 2, 2, DI, 128], BF16)
        WR3 = sb("WR3", [128, DI, 128], F32)
        RZB = sb("RZB", [128, 2, SL], F32)
        RZS = sb("RZS", [128, SL], F32)
        AFB = sb("AF", [128, SL], F32)
        XTC = sb("XTC", [128, 2, SL], F32)
        YST = sb("YST", [128, 2, SL], F32)

        ps_tp = ps_tp_h[:, :]
        ps_k = ps_k_h[:, :]
        ps_v = ps_v_h[:, :]
        ps_kv = ps_kv_h[:, :]
        ps_tp3 = ps_tp.rearrange("p (a m) -> p a m", a=DI)

        def kvcol(p):
            return 512 * (p // 3) + 129 * (p % 3)

        def xn_tile(c, d, hl):
            if c < NOWN:
                t = (XNH, XNL)[hl]
                return t[:, d, 128 * c:128 * (c + 1)]
            return XNB[:, c % 2, hl, d, :]

        def xn_dst(c, hl):
            if c < NOWN:
                t = (XNH, XNL)[hl]
                return t[:, 0:DI, 128 * c:128 * (c + 1)]
            return XNB[:, c % 2, hl, 0:DI, :]

        SG_SLICE = 2 * NHO + 2 * DO

        @block.sync
        def _(sync):
            sync.dma_start(out=IDT[:], in_=ident[:]).then_inc(Sc, 16)
            sync.dma_start(out=BPR[:], in_=bpair.rearrange("h (p m) -> h p m", p=NP)).then_inc(Sc, 16)
            for c in range(2):
                sync.dma_start(out=XRAW[:, c % 2, :],
                               in_=x_ord[128 * c:128 * (c + 1), :]).then_inc(Sx[c % 2], 16)
            for d in range(DI):
                if d >= 2:
                    sync.wait_ge(V_wp, d - 1)
                sync.dma_start(out=WRAW[:, d % 2, 0:KD],
                               in_=wkt[128 * d:128 * (d + 1), :]).then_inc(Sw[d % 2], 16)
            for c in range(2, NCH):
                sync.wait_ge(V_xn, c - 1)
                sync.dma_start(out=XRAW[:, c % 2, :],
                               in_=x_ord[128 * c:128 * (c + 1), :]).then_inc(Sx[c % 2], 16)
            sync.wait_ge(T_dnh, 2 * NS * DO)
            blocks = [("q", 0)]
            for ss in range(NS):
                if ss + 1 < NS:
                    blocks.append(("q", ss + 1))
                blocks.append(("o", ss))
            rawseq = [(k, ss, d) for (k, ss) in blocks for d in range(DO)]
            for gidx, (k, ss, d) in enumerate(rawseq):
                if gidx >= 1:
                    pk, pss, pd = rawseq[gidx - 1]
                    sync.wait_ge(V_wq if pk == "q" else V_wo, pss * DO + pd + 1)
                w = wqt if k == "q" else wot
                sync.dma_start(out=WR3[:],
                               in_=w[:, 128 * d:128 * (d + 1)].rearrange(
                                   "(a p) m -> p a m", p=128)).then_inc(Sw[0], 16)
                if k == "o":
                    idx = ss * DO + d
                    if idx >= 2:
                        sync.wait_ge(V_y, idx - 1)
                    sync.dma_start(out=XTC[:, idx % 2, :],
                                   in_=x_t[128 * d:128 * (d + 1), SL * ss:SL * (ss + 1)]).then_inc(Sxt[idx % 2], 16)
            sync.wait_ge(Sy[0], 16 * NS * DO // 2)
            sync.wait_ge(Sy[1], 16 * NS * DO // 2)

        @block.gpsimd
        def _(gpsimd):
            gpsimd.dma_start(out=WVH[:], in_=wvt.rearrange(
                "(a p) m -> p a m", p=128)).then_inc(Sck[1], 16)
            for d in range(DI):
                if d >= 2:
                    gpsimd.wait_ge(V_wp, DI + d - 1)
                gpsimd.dma_start(out=WRAW[:, d % 2, KD:2 * KD],
                                 in_=wvt[128 * d:128 * (d + 1), :]).then_inc(Swv[d % 2], 16)
            if debug:
                gpsimd.wait_ge(V_kt, 1)
                gpsimd.dma_start(out=dbg["dbg_ktm"][:], in_=KTM[:, 0, :]).then_inc(Sd, 16)
                gpsimd.wait_ge(V_va, 1)
                gpsimd.dma_start(out=dbg["dbg_vkv"][:],
                                 in_=VKV[:, 0, :, :].rearrange("p a m -> p (a m)")).then_inc(Sd, 16)
            for s in range(NS):
                for h in range(NHO):
                    idx = s * NHO + h
                    if idx == 0:
                        gpsimd.wait_ge(V_wp, 2 * DI)
                    if idx == 2:
                        gpsimd.wait_ge(V_xn, NCH)
                    if idx == 2:
                        gpsimd.wait_ge(V_stg, 1)
                        gpsimd.dma_start(out=cc_in[:],
                                         in_=KVS.rearrange("p a m -> p (a m)")).then_inc(Scc_in, 16)
                        gpsimd.wait_ge(Scc_in, 16)
                        gpsimd.collective_compute(
                            "AllGather", ALU.bypass,
                            replica_groups=[[0, 1], [2, 3], [4, 5], [6, 7]],
                            ins=[cc_in[:]], outs=[cc_out[:]]).then_inc(CCs, 1)
                        gpsimd.wait_ge(CCs, 1)
                        gpsimd.dma_start(out=CCB[:],
                                         in_=cc_out.rearrange("r p m -> p r m").rearrange(
                                             "p r (a m) -> p r a m", a=NPL)).then_inc(Scc, 16)
                    if idx >= 3:
                        gpsimd.wait_ge(T_ga, idx - 2)
                    gpsimd.dma_start(out=WGU[:, idx % 3, 0, :, :],
                                     in_=wgt_d[:, 128 * h:128 * (h + 1)].rearrange(
                                         "(a p) m -> p a m", p=128)).then_inc(Sgg[idx % 3], 16)
                    if idx >= 3:
                        gpsimd.wait_ge(T_up, idx - 2)
                    gpsimd.dma_start(out=WGU[:, idx % 3, 1, :, :],
                                     in_=wut_d[:, 128 * h:128 * (h + 1)].rearrange(
                                         "(a p) m -> p a m", p=128)).then_inc(Sgu[idx % 3], 16)
                for d in range(DO):
                    for hf in range(2):
                        j = (s * DO + d) * 2 + hf
                        if j < 2:
                            gpsimd.wait_ge(V_xn, NCH)
                        else:
                            gpsimd.wait_ge(T_dnh, j - 1)
                        gpsimd.dma_start(out=WDT[:, j % 2, :, :],
                                         in_=wdt_d[1536 * hf:1536 * (hf + 1),
                                                   128 * d:128 * (d + 1)].rearrange(
                                             "(a p) m -> p a m", p=128)).then_inc(Sgd[j % 2], 16)
            for s in range(NS):
                for d in range(DO):
                    idx = s * DO + d
                    gpsimd.wait_ge(V_y, idx + 1)
                    gpsimd.dma_start(out=y[128 * d:128 * (d + 1), SL * s:SL * (s + 1)],
                                     in_=YST[:, idx % 2, :]).then_inc(Sy[idx % 2], 16)
            gpsimd.wait_ge(Sy[0], 16 * NS * DO // 2)
            gpsimd.wait_ge(Sy[1], 16 * NS * DO // 2)
            if debug:
                gpsimd.wait_ge(Sd, 32)

        @block.vector
        def _(vector):
            vector.memset(VKV[:, 0:2, 0:NPL, 128:129], 1.0)
            vector.memset(ATL[:], 0.0)
            vector.memset(KSE[:], 0.0)
            for c in range(NCH + 2):
                cb = c % 2
                if c < NCH:
                    vector.wait_ge(Sx[c % 2], 16 * (c // 2 + 1))
                    vector.reduce_sum(ST[:, cb, 0:1], XRAW[:, cb, :], axis=AX.X)
                    vector.wait_ge(A_sq, c + 1)
                    vector.reduce_sum(ST[:, cb, 1:2], XNF[:, cb, :], axis=AX.X)
                    vector.drain()
                    vector.tensor_single_scalar(ST[:, cb, 2:3], ST[:, cb, 0:1], -1.0 / D, ALU.mult)
                    vector.tensor_single_scalar(ST[:, cb, 3:4], ST[:, cb, 1:2], 1.0 / D, ALU.mult)
                    vector.drain()
                    vector.tensor_tensor(ST[:, cb, 4:5], ST[:, cb, 2:3], ST[:, cb, 2:3], ALU.mult)
                    vector.drain()
                    vector.tensor_scalar(ST[:, cb, 5:6], ST[:, cb, 3:4], ST[:, cb, 4:5],
                                         EPS_LN, ALU.subtract, ALU.add).then_inc(V_st, 1)
                    vector.wait_ge(A_rs, c + 1)
                    vector.tensor_scalar(XNF[:, cb, :], XRAW[:, cb, :], ST[:, cb, 2:3],
                                         ST[:, cb, 7:8], ALU.add, ALU.mult).then_inc(V_xn, 1)
                if c < 2:
                    for d in range(DI):
                        idx = c * DI + d
                        if c == 0:
                            vector.wait_ge(Sw[d % 2], 16 * (d // 2 + 1))
                            vector.tensor_copy(WKH[:, d, :], WRAW[:, d % 2, 0:KD])
                            vector.drain()
                            vector.tensor_tensor(WKL[:, d, :], WRAW[:, d % 2, 0:KD],
                                                 WKH[:, d, :], ALU.subtract).then_inc(V_wp, 1)
                        else:
                            vector.wait_ge(Swv[d % 2], 16 * (d // 2 + 1))
                            if d == 0:
                                vector.wait_ge(Sck[1], 16)
                            vector.tensor_tensor(WVL[:, d, :], WRAW[:, d % 2, KD:2 * KD],
                                                 WVH[:, d, :], ALU.subtract).then_inc(V_wp, 1)
                if 1 <= c <= NCH:
                    cl = c - 1
                    vector.wait_ge(A_hi, cl + 1)
                    vector.tensor_tensor(xn_dst(cl, 1), ps_tp3,
                                         xn_dst(cl, 0), ALU.subtract).then_inc(V_lo, 1)
                if 2 <= c < NCH + 2:
                    ck = c - 2
                    if debug and ck == 2:
                        vector.wait_ge(Sd, 32)
                    vector.wait_ge(T_v, ck + 1)
                    vector.tensor_copy(VKV[:, ck % 2, 0:NPL, 0:128],
                                       ps_v.rearrange("p (a m) -> p a m", a=NPL)).then_inc(V_va, 1)
                    vector.wait_ge(A_ke, ck + 1)
                    if ck >= 2:
                        vector.wait_ge(T_kv, ck)
                    vector.scalar_tensor_tensor(KTM[:, ck % 2, :], KEB[:], 1.0, KRB[:],
                                                ALU.min, ALU.max).then_inc(V_kt, 1)
            vector.wait_ge(T_kv, NCH)
            vector.tensor_copy(KVS[:, 0:3, :],
                               ps_kv[:, 0:387].rearrange("p (a m) -> p a m", a=3))
            vector.tensor_copy(KVS[:, 3, :], ps_kv[:, 512:641]).then_inc(V_stg, 1)
            for s in range(NS):
                for h in range(NHO):
                    idx = s * NHO + h
                    if s == 0 and h == 2:
                        vector.wait_ge(Scc, 16)
                        last = None
                        for p in range(NP):
                            r, lp = p // NPL, p % NPL
                            vector.tensor_copy(ATL[0:64, p, 0:64], CCB[0:64, r, lp, 0:64])
                            vector.tensor_copy(ATL[64:128, p, 64:128],
                                               CCB[64:128, r, lp, 64:128])
                            vector.tensor_copy(KSE[0:64, p, 2 * p:2 * p + 1],
                                               CCB[0:64, r, lp, 128:129])
                            last = vector.tensor_copy(KSE[64:128, p, 2 * p + 1:2 * p + 2],
                                                      CCB[64:128, r, lp, 128:129])
                        last.then_inc(V_asm, 1)
                    upp = (ps_v, ps_kv[:, 0:512])[idx % 2]
                    vector.wait_ge(T_up, idx + 1)
                    vector.wait_ge(A_gs, idx + 1)
                    vector.drain()
                    vector.scalar_tensor_tensor(HFB[:], upp, QSCALE, GSIL[:],
                                                ALU.mult, ALU.mult).then_inc(V_hf, 1)
                    vector.wait_ge(A_hh, idx + 1)
                    vector.drain()
                    vector.tensor_tensor(HLO[:, h, :], HFB[:], HHI[:, h, :],
                                         ALU.subtract).then_inc(V_hl, 1)
            RAWG = {}
            blocks = [("q", 0)]
            for ss in range(NS):
                if ss + 1 < NS:
                    blocks.append(("q", ss + 1))
                blocks.append(("o", ss))
            gi = 0
            for (k, ss) in blocks:
                for d in range(DO):
                    RAWG[(k, ss, d)] = gi
                    gi += 1

            def _wq_split(s, d):
                qidx = s * DO + d
                g = RAWG[("q", s, d)]
                vector.wait_ge(Sw[0], 16 * (4 + g + 1))
                if d >= 2:
                    vector.wait_ge(T_q, qidx - 1)
                elif s >= 1:
                    vector.wait_ge(T_q, (s - 1) * DO + d % 2 + 7)
                vector.tensor_copy(WQT[:, d % 2, 0, :, :], WR3[:])
                vector.drain()
                vector.tensor_tensor(WQT[:, d % 2, 1, :, :], WR3[:],
                                     WQT[:, d % 2, 0, :, :], ALU.subtract).then_inc(V_wq, 1)

            def _q_fin(s, d):
                idx = s * DO + d
                vector.wait_ge(A_qe, idx + 1)
                vector.scalar_tensor_tensor(QSL[:, d, :], SCR[:], 1.0,
                                            QSL[:, d, :], ALU.min,
                                            ALU.max).then_inc(V_q, 1)

            def _rz(s):
                vector.wait_ge(T_z, s + 1)
                vector.tensor_single_scalar(AFB[0:16, :], ps_v[0:16, :], EPS_Z, ALU.add)
                vector.drain()
                vector.reciprocal(RZS[0:16, :], AFB[0:16, :]).then_inc(V_rz, 1)

            def _attn_evac(s, p):
                idx = s * NP + p
                att_ps = (ps_kv[:, 0:512], ps_kv[:, 512:1024])[p % 2]
                vector.wait_ge(A_rzs, idx + 1)
                vector.wait_ge(T_at, idx + 1)
                if idx >= 1:
                    vector.wait_ge(A_ath, idx)
                vector.drain()
                vector.tensor_tensor(AFB[:], att_ps, RZB[:, p % 2, :],
                                     ALU.mult).then_inc(V_af, 1)
                vector.wait_ge(A_ath, idx + 1)
                vector.drain()
                vector.tensor_tensor(ATNL[:, p, :], AFB[:], ATNH[:, p, :],
                                     ALU.subtract).then_inc(V_at, 1)

            def _wo_split(s, d):
                oidx = s * DO + d
                g = RAWG[("o", s, d)]
                vector.wait_ge(Sw[0], 16 * (4 + g + 1))
                if d >= 2:
                    vector.wait_ge(T_s, oidx - 1)
                elif s >= 1:
                    vector.wait_ge(T_s, (s - 1) * DO + d % 2 + 7)
                vector.tensor_copy(WOT[:, d % 2, 0, :, :], WR3[:])
                vector.drain()
                vector.tensor_tensor(WOT[:, d % 2, 1, :, :], WR3[:],
                                     WOT[:, d % 2, 0, :, :], ALU.subtract).then_inc(V_wo, 1)

            def _sg(s, d):
                idx = s * DO + d
                s_ps = (ps_tp[:, 0:512], ps_tp[:, 512:1024])[d % 2]
                vector.wait_ge(T_s, idx + 1)
                if d == 0:
                    vector.wait_ge(A_ath, (s + 1) * NP)
                vector.drain()
                vector.tensor_tensor(AFB[:], s_ps, GFM[:, d, SL * s:SL * (s + 1)], ALU.mult)
                vector.drain()
                vector.wait_ge(Sxt[idx % 2], 16 * (idx // 2 + 1))
                if idx >= 2:
                    vector.wait_ge(Sy[idx % 2], 16 * (idx // 2))
                vector.tensor_tensor(YST[:, idx % 2, :], AFB[:], XTC[:, idx % 2, :],
                                     ALU.add).then_inc(V_y, 1)

            for d in range(DO):
                _wq_split(0, d)
                if d >= 2:
                    _q_fin(0, d - 2)
            _q_fin(0, DO - 2)
            _q_fin(0, DO - 1)
            _rz(0)
            for s in range(NS):
                if s + 1 < NS:
                    _wq_split(s + 1, 0)
                    _wq_split(s + 1, 1)
                for p in range(NP):
                    _attn_evac(s, p)
                    if s + 1 < NS:
                        if p + 2 < DO:
                            _wq_split(s + 1, p + 2)
                        if p >= 2:
                            _q_fin(s + 1, p - 2)
                if s + 1 < NS:
                    _q_fin(s + 1, DO - 2)
                    _q_fin(s + 1, DO - 1)
                _wo_split(s, 0)
                _wo_split(s, 1)
                if s + 1 < NS:
                    _rz(s + 1)
                for d in range(2, DO):
                    _wo_split(s, d)
                    _sg(s, d - 2)
                _sg(s, DO - 2)
                _sg(s, DO - 1)

        @block.scalar
        def _(scalar):
            for c in range(NCH + 1):
                cb = c % 2
                if c < NCH:
                    scalar.wait_ge(Sx[c % 2], 16 * (c // 2 + 1))
                    if c >= 2:
                        scalar.wait_ge(T_tp, c - 1)
                    scalar.activation(XNF[:, cb, :], XRAW[:, cb, :],
                                      AF.Square).then_inc(A_sq, 1)
                    scalar.wait_ge(V_st, c + 1)
                    scalar.activation(ST[:, cb, 6:7], ST[:, cb, 5:6], AF.Ln)
                    scalar.drain()
                    scalar.activation(ST[:, cb, 7:8], ST[:, cb, 6:7], AF.Exp,
                                      scale=-0.5).then_inc(A_rs, 1)
                if 1 <= c <= NCH:
                    cl = c - 1
                    scalar.wait_ge(T_tp, cl + 1)
                    scalar.activation(xn_dst(cl, 0), ps_tp3, AF.Copy).then_inc(A_hi, 1)
                    scalar.wait_ge(T_k, cl + 1)
                    if cl >= 1:
                        scalar.wait_ge(V_kt, cl)
                    kp = (ps_k[:, 0:512], ps_k[:, 512:1024])[cl % 2]
                    scalar.activation(KEB[:], kp, AF.Exp)
                    scalar.activation(KRB[:], kp, AF.Relu, bias=1.0).then_inc(A_ke, 1)
            for s in range(NS):
                for h in range(NHO):
                    idx = s * NHO + h
                    gp = (ps_tp[:, 0:512], ps_tp[:, 512:1024])[idx % 2]
                    scalar.wait_ge(T_ga, idx + 1)
                    if idx >= 1:
                        scalar.wait_ge(V_hf, idx)
                    scalar.activation(GSIL[:], gp, AF.Silu, scale=QSCALE).then_inc(A_gs, 1)
                    scalar.wait_ge(V_hf, idx + 1)
                    scalar.activation(HHI[:, h, :], HFB[:], AF.Copy).then_inc(A_hh, 1)
                for d in range(DO):
                    idx = s * DO + d
                    gpp = (ps_kv[:, 512:1024], ps_kv[:, 1024:1536])[idx % 2]
                    scalar.wait_ge(T_dnh, 2 * idx + 2)
                    scalar.activation(GFM[:, d, SL * s:SL * (s + 1)], gpp, AF.Silu,
                                      scale=QSCALE).then_inc(A_g, 1)
            def _qe(s, d):
                idx = s * DO + d
                qp = (ps_k[:, 0:512], ps_k[:, 512:1024],
                      ps_tp[:, 0:512], ps_tp[:, 512:1024])[idx % 4]
                scalar.wait_ge(T_q, idx + 1)
                if s >= 1:
                    scalar.wait_ge(T_at, (s - 1) * DO + d + 1)
                if idx >= 1:
                    scalar.wait_ge(V_q, idx)
                scalar.activation(SCR[:], qp, AF.Exp)
                scalar.activation(QSL[:, d, :], qp, AF.Relu,
                                  bias=1.0).then_inc(A_qe, 1)

            for d in range(DO):
                _qe(0, d)
            for s in range(NS):
                for p in range(NP):
                    idx = s * NP + p
                    scalar.wait_ge(T_rb, idx + 1)
                    if idx >= 2:
                        scalar.wait_ge(V_af, idx - 1)
                    scalar.drain()
                    scalar.activation(RZB[:, p % 2, :], ps_kv[:, 1024:1536],
                                      AF.Copy).then_inc(A_rzs, 1)
                    scalar.wait_ge(V_af, idx + 1)
                    scalar.activation(ATNH[:, p, :], AFB[:], AF.Copy).then_inc(A_ath, 1)
                    if s + 1 < NS and p >= 2:
                        _qe(s + 1, p - 2)
                if s + 1 < NS:
                    _qe(s + 1, DO - 2)
                    _qe(s + 1, DO - 1)
            if debug:
                scalar.wait_ge(Sy[0], 16 * NS * DO // 2)
                scalar.wait_ge(Sy[1], 16 * NS * DO // 2)
                n = 0
                for name, src in [
                    ("dbg_xnh", XNH.rearrange("p a m -> p (a m)")),
                    ("dbg_xnl", XNL.rearrange("p a m -> p (a m)")),
                    ("dbg_atl", ATL.rearrange("p a m -> p (a m)")),
                    ("dbg_kse", KSE.rearrange("p a m -> p (a m)")),
                    ("dbg_g", GFM.rearrange("p a m -> p (a m)")),
                    ("dbg_q", QSL.rearrange("p a m -> p (a m)")),
                    ("dbg_ah", ATNH.rearrange("p a m -> p (a m)")),
                    ("dbg_al", ATNL.rearrange("p a m -> p (a m)")),
                ]:
                    scalar.dma_start(out=dbg[name][:], in_=src).then_inc(Sc, 16)
                    n += 1
                scalar.wait_ge(Sc, 32 + 16 * n)

        @block.tensor
        def _(tensor):
            tensor.wait_ge(Sc, 32)

            def transpose_chunk(c):
                tensor.wait_ge(V_xn, c + 1)
                i = None
                for j in range(8):
                    i = tensor.transpose(ps_tp[:, 128 * j:128 * (j + 1)],
                                         XNF[:, c % 2, 128 * j:128 * (j + 1)], IDT[:])
                i.then_inc(T_tp, 1)

            transpose_chunk(0)

            def kv_mm(c):
                tensor.wait_ge(V_kt, c + 1)
                tensor.wait_ge(V_va, c + 1)
                for p in range(NPL):
                    c0 = kvcol(p)
                    i = tensor.matmul(ps_kv[:, c0:c0 + 129],
                                      KTM[:, c % 2, 128 * p:128 * (p + 1)],
                                      VKV[:, c % 2, p, :], start=(c == 0 and p % 3 == 0),
                                      stop=(c == NCH - 1),
                                      skip_group_check=True)
                i.then_inc(T_kv, 1)

            for c in range(NCH):
                tensor.wait_ge(A_hi, c + 1)
                tensor.wait_ge(V_lo, c + 1)
                if c >= 1:
                    tensor.wait_ge(A_ke, c - 1)
                kp = (ps_k[:, 0:512], ps_k[:, 512:1024])[c % 2]
                first = True
                for d in range(DI):
                    if c == 0:
                        tensor.wait_ge(V_wp, d + 1)
                    for hl, W in ((0, WKH), (0, WKL), (1, WKH)):
                        i = tensor.matmul(kp, xn_tile(c, d, hl), W[:, d, :],
                                          start=first,
                                          stop=(d == DI - 1 and hl == 1),
                                          skip_group_check=True)
                        first = False
                i.then_inc(T_k, 1)
                if c + 1 < NCH:
                    transpose_chunk(c + 1)
                if c >= 1:
                    tensor.wait_ge(V_va, c)
                for d in range(DI):
                    if c == 0:
                        tensor.wait_ge(V_wp, DI + d + 1)
                    for hl, W in ((0, WVH), (0, WVL), (1, WVH)):
                        i = tensor.matmul(ps_v, xn_tile(c, d, hl), W[:, d, :],
                                          start=(d == 0 and hl == 0 and W is WVH),
                                          stop=(d == DI - 1 and hl == 1),
                                          skip_group_check=True)
                i.then_inc(T_v, 1)
                if c >= 1:
                    kv_mm(c - 1)
            kv_mm(NCH - 1)

            for s in range(NS):
                for h in range(NHO):
                    idx = s * NHO + h
                    gp = (ps_tp[:, 0:512], ps_tp[:, 512:1024])[idx % 2]
                    upp = (ps_v, ps_kv[:, 0:512])[idx % 2]
                    tensor.wait_ge(Sgg[idx % 3], 16 * (idx // 3 + 1))
                    if idx >= 2:
                        tensor.wait_ge(A_gs, idx - 1)
                    for d in range(DI):
                        for X in (XNH, XNL):
                            i = tensor.matmul(gp, WGU[:, idx % 3, 0, d, :],
                                              X[:, d, SL * s:SL * (s + 1)],
                                              start=(d == 0 and X is XNH),
                                              stop=(d == DI - 1 and X is XNL),
                                              skip_group_check=True)
                    i.then_inc(T_ga, 1)
                    tensor.wait_ge(Sgu[idx % 3], 16 * (idx // 3 + 1))
                    if idx == 1:
                        tensor.wait_ge(V_stg, 1)
                    if idx >= 2:
                        tensor.wait_ge(V_hf, idx - 1)
                    for d in range(DI):
                        for X in (XNH, XNL):
                            i = tensor.matmul(upp, WGU[:, idx % 3, 1, d, :],
                                              X[:, d, SL * s:SL * (s + 1)],
                                              start=(d == 0 and X is XNH),
                                              stop=(d == DI - 1 and X is XNL),
                                              skip_group_check=True)
                    i.then_inc(T_up, 1)
                tensor.wait_ge(V_hl, s * NHO + NHO)
                for d in range(DO):
                    idx = s * DO + d
                    gpp = (ps_kv[:, 512:1024], ps_kv[:, 1024:1536])[idx % 2]
                    if idx == 0:
                        tensor.wait_ge(V_stg, 1)
                    if idx >= 2:
                        tensor.wait_ge(A_g, idx - 1)
                    for hf in range(2):
                        j = idx * 2 + hf
                        tensor.wait_ge(Sgd[j % 2], 16 * (j // 2 + 1))
                        for h in range(12):
                            hh = 12 * hf + h
                            for HB in (HHI, HLO):
                                i = tensor.matmul(gpp, WDT[:, j % 2, h, :], HB[:, hh, :],
                                                  start=(hf == 0 and h == 0 and HB is HHI),
                                                  stop=(hf == 1 and h == 11 and HB is HLO),
                                                  skip_group_check=True)
                        i.then_inc(T_dnh, 1)

            tensor.wait_ge(A_g, NS * DO)
            tensor.wait_ge(V_asm, 1)
            tensor.wait_ge(A_ke, NCH)
            tensor.wait_ge(V_hf, NS * NHO)
            tensor.wait_ge(A_gs, NS * NHO)

            def _zmm(s, d):
                idx = s * DO + d
                tensor.wait_ge(V_q, idx + 1)
                i = tensor.matmul(ps_v[0:16, :], KSE[:, d, :], QSL[:, d, :],
                                  start=(d == 0), stop=(d == DO - 1),
                                  skip_group_check=True)
                if d == DO - 1:
                    i.then_inc(T_z, 1)

            def _qblock(s, d):
                idx = s * DO + d
                qp = (ps_k[:, 0:512], ps_k[:, 512:1024],
                      ps_tp[:, 0:512], ps_tp[:, 512:1024])[idx % 4]
                tensor.wait_ge(V_wq, idx + 1)
                if idx >= 4:
                    tensor.wait_ge(A_qe, idx - 3)
                if idx % 4 >= 2 and s >= 2:
                    tensor.wait_ge(V_y, (s - 2) * DO + 7 + (idx % 4 - 2))
                i = None
                for dd in range(DI):
                    for hl, src in ((0, 0), (0, 1), (1, 0)):
                        i = tensor.matmul(qp, WQT[:, d % 2, hl, dd, :],
                                          (XNH, XNL)[src][:, dd, SL * s:SL * (s + 1)],
                                          start=(dd == 0 and hl == 0 and src == 0),
                                          stop=(dd == DI - 1 and hl == 1),
                                          skip_group_check=True)
                i.then_inc(T_q, 1)

            def _qphase(s):
                for d in range(DO):
                    _qblock(s, d)
                    if d >= 4:
                        _zmm(s, d - 4)
                for d in range(DO - 4, DO):
                    _zmm(s, d)

            def _pair(s, p):
                idx = s * NP + p
                if idx >= 1:
                    tensor.wait_ge(A_rzs, idx)
                i = tensor.matmul(ps_kv[:, 1024:1536], BPR[:, p, :], RZS[0:16, :],
                                  start=True, stop=True, skip_group_check=True)
                i.then_inc(T_rb, 1)
                att_ps = (ps_kv[:, 0:512], ps_kv[:, 512:1024])[p % 2]
                if idx >= 2:
                    tensor.wait_ge(V_af, idx - 1)
                i = tensor.matmul(att_ps, ATL[:, p, :], QSL[:, p, :],
                                  start=True, stop=True, skip_group_check=True)
                i.then_inc(T_at, 1)

            _qphase(0)
            for s in range(NS):
                tensor.wait_ge(V_rz, s + 1)
                _pair(s, 0)
                _pair(s, 1)
                if s + 1 < NS:
                    for j in range(DO):
                        _qblock(s + 1, j)
                        if j + 2 < NP:
                            _pair(s, j + 2)
                    for d in range(DO):
                        _zmm(s + 1, d)
                else:
                    for p in range(2, NP):
                        _pair(s, p)
                for d in range(DO):
                    idx = s * DO + d
                    sp = (ps_tp[:, 0:512], ps_tp[:, 512:1024])[d % 2]
                    tensor.wait_ge(V_wo, idx + 1)
                    if d == 0 and s + 1 < NS:
                        tensor.wait_ge(A_qe, (s + 2) * DO)
                    if idx >= 2:
                        tensor.wait_ge(V_y, idx - 1)
                    for e in range(DO):
                        if d == 0:
                            tensor.wait_ge(V_at, s * NP + e + 1)
                        for hl, src in ((0, 0), (0, 1), (1, 0)):
                            i = tensor.matmul(sp, WOT[:, d % 2, hl, e, :],
                                              (ATNH, ATNL)[src][:, e, :],
                                              start=(e == 0 and hl == 0 and src == 0),
                                              stop=(e == DO - 1 and hl == 1),
                                              skip_group_check=True)
                    i.then_inc(T_s, 1)

    return nc


_NC_CACHE = {}


def _get_nc(debug):
    if debug not in _NC_CACHE:
        _NC_CACHE[debug] = build_nc(debug)
    return _NC_CACHE[debug]


def kernel(L0_attn_x, L0_attn_Wq, L0_attn_Wk, L0_attn_Wv, L0_attn_Wo,
           L0_ffn_w_gate, L0_ffn_w_up, L0_ffn_w_down, _debug=False):
    x = np.ascontiguousarray(np.asarray(L0_attn_x, dtype=np.float32))
    wq = np.ascontiguousarray(np.asarray(L0_attn_Wq, dtype=np.float32).T)
    wk = np.ascontiguousarray(np.asarray(L0_attn_Wk, dtype=np.float32).T)
    wv = np.ascontiguousarray(np.asarray(L0_attn_Wv, dtype=np.float32).T)
    wo = np.ascontiguousarray(np.asarray(L0_attn_Wo, dtype=np.float32).T)
    wg = np.ascontiguousarray(np.asarray(L0_ffn_w_gate, dtype=np.int32).T)
    wu = np.ascontiguousarray(np.asarray(L0_ffn_w_up, dtype=np.int32).T)
    wd = np.ascontiguousarray(np.asarray(L0_ffn_w_down, dtype=np.int32).T)
    ident = np.eye(128, dtype=np.float32)
    bpair = np.zeros((16, NP * 128), dtype=np.float32)
    for p in range(NP):
        bpair[2 * p, p * 128:p * 128 + 64] = 1.0
        bpair[2 * p + 1, p * 128 + 64:p * 128 + 128] = 1.0

    nc = _get_nc(_debug)
    in_maps = []
    for c in range(8):
        b, h = c // 2, c % 2
        own = x[b, 2048 * h:2048 * (h + 1), :]
        nb = x[b, 2048 * (1 - h):2048 * (2 - h), :]
        x_ord = np.ascontiguousarray(np.concatenate([own, nb], axis=0))
        x_tp = np.ascontiguousarray(own.T)
        wk_c = np.ascontiguousarray(wk[:, 512 * h:512 * (h + 1)])
        wv_c = np.ascontiguousarray(wv[:, 512 * h:512 * (h + 1)])
        in_maps.append({
            "x_ord": x_ord, "x_t": x_tp,
            "wqt": wq, "wkt": wk_c, "wvt": wv_c, "wot": wo,
            "wgt": wg, "wut": wu, "wdt": wd,
            "ident": ident, "bpair": bpair,
        })
    res = run_bass_kernel_spmd(nc, in_maps, core_ids=list(range(8)))
    out = np.empty((4, 4096, D), dtype=np.float32)
    for c in range(8):
        b, h = c // 2, c % 2
        out[b, 2048 * h:2048 * (h + 1), :] = res.results[c]["y"].T
    if _debug:
        return out, res
    return out


# version 21 (final)
# speedup vs baseline: 1.0772x; 1.0772x over previous
import numpy as np

import concourse.bass as bass
import concourse.mybir as mybir
from concourse.bass_utils import run_bass_kernel_spmd

F32 = mybir.dt.float32
BF16 = mybir.dt.bfloat16
I32 = mybir.dt.int32
AF = mybir.ActivationFunctionType
ALU = mybir.AluOpType
AX = mybir.AxisListType

D = 1024
HID = 3072
TOK = 2048
NCH = 32
NOWN = 16
DI = 8
DO = 8
NHO = 24
NP = 8
NPL = 4
KD = 512
NS = 4
SL = 512
EPS_LN = 1e-5
EPS_Z = 1e-6
QSCALE = 1.0 / 127.0


def _off():
    o = {}
    o["XNH"] = 0
    o["XNL"] = 32768
    o["WKH"] = 65536
    o["WKL"] = 73728
    o["WVH"] = 81920
    o["WVL"] = 90112
    o["GFM"] = 65536
    o["ATL"] = 131072
    o["KSE"] = 135168
    o["IDT"] = 135680
    o["BPR"] = 136192
    base = 140288
    o["WRAW"] = base
    o["XRAW"] = base + 8192
    o["XNF"] = base + 16384
    o["XNB"] = base + 24576
    o["KEB"] = base + 32768
    o["KRB"] = base + 36864
    o["KTM"] = base + 40960
    o["VKV"] = base + 49152
    o["ST"] = base + 57408
    o["WGU"] = base
    o["WUT_unused"] = base + 4096
    o["WDT"] = base + 12288
    o["GSIL"] = base + 18432
    o["HFB"] = base + 20480
    o["HHI"] = base + 22528
    o["HLO"] = base + 47104
    o["QSL"] = base
    o["ATNH"] = base + 16384
    o["ATNL"] = base + 24576
    o["WQT"] = base + 32768
    o["WOT"] = base + 40960
    o["WR3"] = base + 49152
    o["RZB"] = base + 53248
    o["RZS"] = base + 57344
    o["AF"] = base + 59392
    o["XTC"] = base + 61440
    o["YST"] = base + 65536
    o["KVS"] = base + 53248
    o["CCB"] = base + 61440
    o["SCR"] = base + 69632
    return o


def build_nc(debug=False):
    nc = bass.Bass("TRN2", target_bir_lowering=False)

    x_ord = nc.dram_tensor("x_ord", [4096, D], F32, kind="ExternalInput").ap()
    x_t = nc.dram_tensor("x_t", [D, TOK], F32, kind="ExternalInput").ap()
    wqt = nc.dram_tensor("wqt", [D, D], F32, kind="ExternalInput").ap()
    wkt = nc.dram_tensor("wkt", [D, KD], F32, kind="ExternalInput").ap()
    wvt = nc.dram_tensor("wvt", [D, KD], F32, kind="ExternalInput").ap()
    cc_in = nc.dram_tensor("cc_in", [128, NPL * 129], F32).ap()
    cc_out = nc.dram_tensor("cc_out", [2, 128, NPL * 129], F32).ap()
    wot = nc.dram_tensor("wot", [D, D], F32, kind="ExternalInput").ap()
    wgt_d = nc.dram_tensor("wgt", [D, HID], I32, kind="ExternalInput").ap()
    wut_d = nc.dram_tensor("wut", [D, HID], I32, kind="ExternalInput").ap()
    wdt_d = nc.dram_tensor("wdt", [HID, D], I32, kind="ExternalInput").ap()
    ident = nc.dram_tensor("ident", [128, 128], F32, kind="ExternalInput").ap()
    bpair = nc.dram_tensor("bpair", [16, NP * 128], F32, kind="ExternalInput").ap()
    y = nc.dram_tensor("y", [D, TOK], F32, kind="ExternalOutput").ap()

    dbg = {}
    if debug:
        for name, shape, dt in [
            ("dbg_xnh", [128, DI * TOK], BF16), ("dbg_xnl", [128, DI * TOK], BF16),
            ("dbg_atl", [128, NP * 128], F32), ("dbg_kse", [128, NP * 16], F32),
            ("dbg_g", [128, DO * TOK], F32), ("dbg_q", [128, NP * SL], F32),
            ("dbg_ah", [128, NP * SL], BF16), ("dbg_al", [128, NP * SL], BF16),
            ("dbg_ktm", [128, KD], F32), ("dbg_vkv", [128, NPL * 129], F32),
        ]:
            dbg[name] = nc.dram_tensor(name, shape, dt, kind="ExternalOutput").ap()

    o = _off()
    ARENA = 207 * 1024

    from contextlib import ExitStack
    with ExitStack() as ctx:
        _arena = ctx.enter_context(nc.sbuf_tensor("arena", [128, ARENA], mybir.dt.uint8))
        ps_tp_h = ctx.enter_context(nc.psum_tensor("ps_tp", [128, 1024], F32))
        ps_k_h = ctx.enter_context(nc.psum_tensor("ps_k", [128, 1024], F32))
        ps_v_h = ctx.enter_context(nc.psum_tensor("ps_v", [128, 512], F32))
        ps_kv_h = ctx.enter_context(nc.psum_tensor("ps_kv", [128, 1536], F32))
        sems = {}
        for sname in ["Sd", "Sck0", "Sck1", "Sx0", "Sx1", "Sw0", "Sw1", "Sgg0", "Sgg1", "Sgu0",
                      "Sgu1", "Sgd0", "Sgd1", "Sxt0", "Sxt1", "Sy0", "Sy1", "Sc",
                      "V_st", "V_xn", "V_lo", "V_wp", "V_kt", "V_va", "V_asm",
                      "V_hf", "V_hl", "V_wq", "V_wo", "V_q", "V_rz", "V_af",
                      "V_at", "V_y",
                      "A_sq", "A_rs", "A_hi", "A_ke", "A_gs", "A_hh", "A_g", "A_qe",
                      "A_rzs", "A_ath",
                      "T_tp", "T_k", "T_v", "T_kv", "T_ga", "T_up", "T_dnh",
                      "T_q", "T_z", "T_rb", "T_at", "T_s", "G_kt",
                      "V_stg", "Scc_in", "Scc", "CCs", "Sgg2", "Sgu2",
                      "Swv0", "Swv1", "Swu0", "Swu1", "Swu2"]:
            sems[sname] = ctx.enter_context(nc.semaphore(sname))
        g = sems
        Sx = (g["Sx0"], g["Sx1"]); Sw = (g["Sw0"], g["Sw1"])
        Sgd = (g["Sgd0"], g["Sgd1"]); Sxt = (g["Sxt0"], g["Sxt1"])
        Sy = (g["Sy0"], g["Sy1"]); Sc = g["Sc"]; Sd = g["Sd"]
        Sck = (g["Sck0"], g["Sck1"]); G_kt = g["G_kt"]
        Sgg = (g["Sgg0"], g["Sgg1"], g["Sgg2"])
        Sgu = (g["Sgu0"], g["Sgu1"], g["Sgu2"])
        Swv = (g["Swv0"], g["Swv1"])
        Swu = (g["Swu0"], g["Swu1"], g["Swu2"])
        V_stg = g["V_stg"]; Scc_in = g["Scc_in"]; Scc = g["Scc"]; CCs = g["CCs"]
        A_sq = g["A_sq"]
        (V_st, V_xn, V_lo, V_wp, V_kt, V_va, V_asm,
         V_hf, V_hl, V_wq, V_wo, V_q, V_rz, V_af, V_at, V_y,
         A_rs, A_hi, A_ke, A_gs, A_hh, A_g, A_qe, A_rzs, A_ath,
         T_tp, T_k, T_v, T_kv, T_ga, T_up, T_dnh, T_q, T_z, T_rb, T_at,
         T_s) = (sems[n] for n in [
                 "V_st", "V_xn", "V_lo", "V_wp", "V_kt", "V_va", "V_asm",
                 "V_hf", "V_hl", "V_wq", "V_wo", "V_q", "V_rz", "V_af",
                 "V_at", "V_y",
                 "A_rs", "A_hi", "A_ke", "A_gs", "A_hh", "A_g", "A_qe",
                 "A_rzs", "A_ath",
                 "T_tp", "T_k", "T_v", "T_kv", "T_ga", "T_up", "T_dnh",
                 "T_q", "T_z", "T_rb", "T_at", "T_s"])
        block = ctx.enter_context(nc.Block())
        slab = nc.lookup_mloc(_arena).addr

        def sb(name, shape, dt):
            return nc.alloc_sbuf_tensor_at(name, shape, dt, offset=slab + o[name]).ap()

        XNH = sb("XNH", [128, DI, TOK], BF16)
        XNL = sb("XNL", [128, DI, TOK], BF16)
        WKH = sb("WKH", [128, DI, KD], BF16)
        WKL = sb("WKL", [128, DI, KD], BF16)
        WVH = sb("WVH", [128, DI, KD], BF16)
        WVL = sb("WVL", [128, DI, KD], BF16)
        GFM = sb("GFM", [128, DO, TOK], F32)
        ATL = sb("ATL", [128, NP, 128], F32)
        KSE = sb("KSE", [128, NP, 16], F32)
        IDT = sb("IDT", [128, 128], F32)
        BPR = sb("BPR", [16, NP, 128], F32)
        WRAW = sb("WRAW", [128, 2, D], F32)
        XRAW = sb("XRAW", [128, 2, D], F32)
        XNF = sb("XNF", [128, 2, D], F32)
        XNB = sb("XNB", [128, 2, 2, DI, 128], BF16)
        KEB = sb("KEB", [128, KD], F32)
        KRB = sb("KRB", [128, KD], F32)
        KTM = sb("KTM", [128, 2, KD], F32)
        VKV = sb("VKV", [128, 2, NPL, 129], F32)
        KVS = sb("KVS", [128, NPL, 129], F32)
        CCB = sb("CCB", [128, 2, NPL, 129], F32)
        SCR = sb("SCR", [128, SL], F32)
        ST = sb("ST", [128, 2, 8], F32)
        WGU = sb("WGU", [128, 3, 2, DI, 128], BF16)
        WDT = sb("WDT", [128, 2, 12, 128], BF16)
        GSIL = sb("GSIL", [128, SL], F32)
        HFB = sb("HFB", [128, SL], F32)
        HHI = sb("HHI", [128, NHO, SL], BF16)
        HLO = sb("HLO", [128, NHO, SL], BF16)
        QSL = sb("QSL", [128, NP, SL], F32)
        ATNH = sb("ATNH", [128, NP, SL], BF16)
        ATNL = sb("ATNL", [128, NP, SL], BF16)
        WQT = sb("WQT", [128, 2, 2, DI, 128], BF16)
        WOT = sb("WOT", [128, 2, 2, DI, 128], BF16)
        WR3 = sb("WR3", [128, DI, 128], F32)
        RZB = sb("RZB", [128, 2, SL], F32)
        RZS = sb("RZS", [128, SL], F32)
        AFB = sb("AF", [128, SL], F32)
        XTC = sb("XTC", [128, 2, SL], F32)
        YST = sb("YST", [128, 2, SL], F32)

        ps_tp = ps_tp_h[:, :]
        ps_k = ps_k_h[:, :]
        ps_v = ps_v_h[:, :]
        ps_kv = ps_kv_h[:, :]
        ps_tp3 = ps_tp.rearrange("p (a m) -> p a m", a=DI)

        def kvcol(p):
            return 512 * (p // 3) + 129 * (p % 3)

        def xn_tile(c, d, hl):
            if c < NOWN:
                t = (XNH, XNL)[hl]
                return t[:, d, 128 * c:128 * (c + 1)]
            return XNB[:, c % 2, hl, d, :]

        def xn_dst(c, hl):
            if c < NOWN:
                t = (XNH, XNL)[hl]
                return t[:, 0:DI, 128 * c:128 * (c + 1)]
            return XNB[:, c % 2, hl, 0:DI, :]

        SG_SLICE = 2 * NHO + 2 * DO

        @block.sync
        def _(sync):
            sync.dma_start(out=IDT[:], in_=ident[:]).then_inc(Sc, 16)
            sync.dma_start(out=BPR[:], in_=bpair.rearrange("h (p m) -> h p m", p=NP)).then_inc(Sc, 16)
            for c in range(2):
                sync.dma_start(out=XRAW[:, c % 2, :],
                               in_=x_ord[128 * c:128 * (c + 1), :]).then_inc(Sx[c % 2], 16)
            for d in range(DI):
                if d >= 2:
                    sync.wait_ge(V_wp, d - 1)
                sync.dma_start(out=WRAW[:, d % 2, 0:KD],
                               in_=wkt[128 * d:128 * (d + 1), :]).then_inc(Sw[d % 2], 16)
            for c in range(2, NCH):
                sync.wait_ge(V_xn, c - 1)
                sync.dma_start(out=XRAW[:, c % 2, :],
                               in_=x_ord[128 * c:128 * (c + 1), :]).then_inc(Sx[c % 2], 16)
            sync.wait_ge(T_dnh, 2 * NS * DO)
            blocks = [("q", 0)]
            for ss in range(NS):
                if ss + 1 < NS:
                    blocks.append(("q", ss + 1))
                blocks.append(("o", ss))
            rawseq = [(k, ss, d) for (k, ss) in blocks for d in range(DO)]
            for gidx, (k, ss, d) in enumerate(rawseq):
                if gidx >= 1:
                    pk, pss, pd = rawseq[gidx - 1]
                    sync.wait_ge(V_wq if pk == "q" else V_wo, pss * DO + pd + 1)
                w = wqt if k == "q" else wot
                sync.dma_start(out=WR3[:],
                               in_=w[:, 128 * d:128 * (d + 1)].rearrange(
                                   "(a p) m -> p a m", p=128)).then_inc(Sw[0], 16)
                if k == "o":
                    idx = ss * DO + d
                    if idx >= 2:
                        sync.wait_ge(V_y, idx - 1)
                    sync.dma_start(out=XTC[:, idx % 2, :],
                                   in_=x_t[128 * d:128 * (d + 1), SL * ss:SL * (ss + 1)]).then_inc(Sxt[idx % 2], 16)
            sync.wait_ge(Sy[0], 16 * NS * DO // 2)
            sync.wait_ge(Sy[1], 16 * NS * DO // 2)

        @block.gpsimd
        def _(gpsimd):
            gpsimd.dma_start(out=WVH[:], in_=wvt.rearrange(
                "(a p) m -> p a m", p=128)).then_inc(Sck[1], 16)
            for d in range(DI):
                if d >= 2:
                    gpsimd.wait_ge(V_wp, DI + d - 1)
                gpsimd.dma_start(out=WRAW[:, d % 2, KD:2 * KD],
                                 in_=wvt[128 * d:128 * (d + 1), :]).then_inc(Swv[d % 2], 16)
            if debug:
                gpsimd.wait_ge(V_kt, 1)
                gpsimd.dma_start(out=dbg["dbg_ktm"][:], in_=KTM[:, 0, :]).then_inc(Sd, 16)
                gpsimd.wait_ge(V_va, 1)
                gpsimd.dma_start(out=dbg["dbg_vkv"][:],
                                 in_=VKV[:, 0, :, :].rearrange("p a m -> p (a m)")).then_inc(Sd, 16)
            for s in range(NS):
                for h in range(NHO):
                    idx = s * NHO + h
                    if idx == 0:
                        gpsimd.wait_ge(V_wp, 2 * DI)
                    if idx == 2:
                        gpsimd.wait_ge(V_xn, NCH)
                    if idx == 4:
                        gpsimd.wait_ge(V_stg, 1)
                        gpsimd.dma_start(out=cc_in[:],
                                         in_=KVS.rearrange("p a m -> p (a m)")).then_inc(Scc_in, 16)
                        gpsimd.wait_ge(Scc_in, 16)
                        gpsimd.collective_compute(
                            "AllGather", ALU.bypass,
                            replica_groups=[[0, 1], [2, 3], [4, 5], [6, 7]],
                            ins=[cc_in[:]], outs=[cc_out[:]]).then_inc(CCs, 1)
                        gpsimd.wait_ge(CCs, 1)
                        gpsimd.dma_start(out=CCB[:],
                                         in_=cc_out.rearrange("r p m -> p r m").rearrange(
                                             "p r (a m) -> p r a m", a=NPL)).then_inc(Scc, 16)
                    if idx >= 3:
                        gpsimd.wait_ge(T_ga, idx - 2)
                    gpsimd.dma_start(out=WGU[:, idx % 3, 0, :, :],
                                     in_=wgt_d[:, 128 * h:128 * (h + 1)].rearrange(
                                         "(a p) m -> p a m", p=128)).then_inc(Sgg[idx % 3], 16)
                    if idx >= 3:
                        gpsimd.wait_ge(T_up, idx - 2)
                    gpsimd.dma_start(out=WGU[:, idx % 3, 1, :, :],
                                     in_=wut_d[:, 128 * h:128 * (h + 1)].rearrange(
                                         "(a p) m -> p a m", p=128)).then_inc(Sgu[idx % 3], 16)
                for d in range(DO):
                    for hf in range(2):
                        j = (s * DO + d) * 2 + hf
                        if j < 2:
                            gpsimd.wait_ge(V_xn, NCH)
                        else:
                            gpsimd.wait_ge(T_dnh, j - 1)
                        gpsimd.dma_start(out=WDT[:, j % 2, :, :],
                                         in_=wdt_d[1536 * hf:1536 * (hf + 1),
                                                   128 * d:128 * (d + 1)].rearrange(
                                             "(a p) m -> p a m", p=128)).then_inc(Sgd[j % 2], 16)
            for s in range(NS):
                for d in range(DO):
                    idx = s * DO + d
                    gpsimd.wait_ge(V_y, idx + 1)
                    gpsimd.dma_start(out=y[128 * d:128 * (d + 1), SL * s:SL * (s + 1)],
                                     in_=YST[:, idx % 2, :]).then_inc(Sy[idx % 2], 16)
            gpsimd.wait_ge(Sy[0], 16 * NS * DO // 2)
            gpsimd.wait_ge(Sy[1], 16 * NS * DO // 2)
            if debug:
                gpsimd.wait_ge(Sd, 32)

        @block.vector
        def _(vector):
            vector.memset(VKV[:, 0:2, 0:NPL, 128:129], 1.0)
            vector.memset(ATL[:], 0.0)
            vector.memset(KSE[:], 0.0)
            for c in range(NCH + 2):
                cb = c % 2
                if c < NCH:
                    vector.wait_ge(Sx[c % 2], 16 * (c // 2 + 1))
                    vector.reduce_sum(ST[:, cb, 0:1], XRAW[:, cb, :], axis=AX.X)
                    vector.wait_ge(A_sq, c + 1)
                    vector.reduce_sum(ST[:, cb, 1:2], XNF[:, cb, :], axis=AX.X)
                    vector.drain()
                    vector.tensor_single_scalar(ST[:, cb, 2:3], ST[:, cb, 0:1], -1.0 / D, ALU.mult)
                    vector.tensor_single_scalar(ST[:, cb, 3:4], ST[:, cb, 1:2], 1.0 / D, ALU.mult)
                    vector.drain()
                    vector.tensor_tensor(ST[:, cb, 4:5], ST[:, cb, 2:3], ST[:, cb, 2:3], ALU.mult)
                    vector.drain()
                    vector.tensor_scalar(ST[:, cb, 5:6], ST[:, cb, 3:4], ST[:, cb, 4:5],
                                         EPS_LN, ALU.subtract, ALU.add).then_inc(V_st, 1)
                    vector.wait_ge(A_rs, c + 1)
                    vector.tensor_scalar(XNF[:, cb, :], XRAW[:, cb, :], ST[:, cb, 2:3],
                                         ST[:, cb, 7:8], ALU.add, ALU.mult).then_inc(V_xn, 1)
                if c < 2:
                    for d in range(DI):
                        idx = c * DI + d
                        if c == 0:
                            vector.wait_ge(Sw[d % 2], 16 * (d // 2 + 1))
                            vector.tensor_copy(WKH[:, d, :], WRAW[:, d % 2, 0:KD])
                            vector.drain()
                            vector.tensor_tensor(WKL[:, d, :], WRAW[:, d % 2, 0:KD],
                                                 WKH[:, d, :], ALU.subtract).then_inc(V_wp, 1)
                        else:
                            vector.wait_ge(Swv[d % 2], 16 * (d // 2 + 1))
                            if d == 0:
                                vector.wait_ge(Sck[1], 16)
                            vector.tensor_tensor(WVL[:, d, :], WRAW[:, d % 2, KD:2 * KD],
                                                 WVH[:, d, :], ALU.subtract).then_inc(V_wp, 1)
                if 1 <= c <= NCH:
                    cl = c - 1
                    vector.wait_ge(A_hi, cl + 1)
                    vector.tensor_tensor(xn_dst(cl, 1), ps_tp3,
                                         xn_dst(cl, 0), ALU.subtract).then_inc(V_lo, 1)
                if 2 <= c < NCH + 2:
                    ck = c - 2
                    if debug and ck == 2:
                        vector.wait_ge(Sd, 32)
                    vector.wait_ge(T_v, ck + 1)
                    vector.tensor_copy(VKV[:, ck % 2, 0:NPL, 0:128],
                                       ps_v.rearrange("p (a m) -> p a m", a=NPL)).then_inc(V_va, 1)
                    vector.wait_ge(A_ke, ck + 1)
                    if ck >= 2:
                        vector.wait_ge(T_kv, ck)
                    vector.scalar_tensor_tensor(KTM[:, ck % 2, :], KEB[:], 1.0, KRB[:],
                                                ALU.min, ALU.max).then_inc(V_kt, 1)
            vector.wait_ge(T_kv, NCH)
            vector.tensor_copy(KVS[:, 0:3, :],
                               ps_kv[:, 0:387].rearrange("p (a m) -> p a m", a=3))
            vector.tensor_copy(KVS[:, 3, :], ps_kv[:, 512:641]).then_inc(V_stg, 1)
            for s in range(NS):
                for h in range(NHO):
                    idx = s * NHO + h
                    if s == 0 and h == 6:
                        vector.wait_ge(Scc_in, 16)
                    if s == 0 and h == 14:
                        vector.wait_ge(Scc, 16)
                        last = None
                        for p in range(NP):
                            r, lp = p // NPL, p % NPL
                            vector.tensor_copy(ATL[0:64, p, 0:64], CCB[0:64, r, lp, 0:64])
                            vector.tensor_copy(ATL[64:128, p, 64:128],
                                               CCB[64:128, r, lp, 64:128])
                            vector.tensor_copy(KSE[0:64, p, 2 * p:2 * p + 1],
                                               CCB[0:64, r, lp, 128:129])
                            last = vector.tensor_copy(KSE[64:128, p, 2 * p + 1:2 * p + 2],
                                                      CCB[64:128, r, lp, 128:129])
                        last.then_inc(V_asm, 1)
                    upp = (ps_v, ps_kv[:, 0:512])[idx % 2]
                    vector.wait_ge(T_up, idx + 1)
                    vector.wait_ge(A_gs, idx + 1)
                    vector.drain()
                    vector.scalar_tensor_tensor(HFB[:], upp, QSCALE, GSIL[:],
                                                ALU.mult, ALU.mult).then_inc(V_hf, 1)
                    vector.wait_ge(A_hh, idx + 1)
                    vector.drain()
                    vector.tensor_tensor(HLO[:, h, :], HFB[:], HHI[:, h, :],
                                         ALU.subtract).then_inc(V_hl, 1)
            RAWG = {}
            blocks = [("q", 0)]
            for ss in range(NS):
                if ss + 1 < NS:
                    blocks.append(("q", ss + 1))
                blocks.append(("o", ss))
            gi = 0
            for (k, ss) in blocks:
                for d in range(DO):
                    RAWG[(k, ss, d)] = gi
                    gi += 1

            def _wq_split(s, d):
                qidx = s * DO + d
                g = RAWG[("q", s, d)]
                vector.wait_ge(Sw[0], 16 * (4 + g + 1))
                if d >= 2:
                    vector.wait_ge(T_q, qidx - 1)
                elif s >= 1:
                    vector.wait_ge(T_q, (s - 1) * DO + d % 2 + 7)
                vector.tensor_copy(WQT[:, d % 2, 0, :, :], WR3[:])
                vector.drain()
                vector.tensor_tensor(WQT[:, d % 2, 1, :, :], WR3[:],
                                     WQT[:, d % 2, 0, :, :], ALU.subtract).then_inc(V_wq, 1)

            def _q_fin(s, d):
                idx = s * DO + d
                vector.wait_ge(A_qe, idx + 1)
                vector.scalar_tensor_tensor(QSL[:, d, :], SCR[:], 1.0,
                                            QSL[:, d, :], ALU.min,
                                            ALU.max).then_inc(V_q, 1)

            def _rz(s):
                vector.wait_ge(T_z, s + 1)
                vector.tensor_single_scalar(AFB[0:16, :], ps_v[0:16, :], EPS_Z, ALU.add)
                vector.drain()
                vector.reciprocal(RZS[0:16, :], AFB[0:16, :]).then_inc(V_rz, 1)

            def _attn_evac(s, p):
                idx = s * NP + p
                att_ps = (ps_kv[:, 0:512], ps_kv[:, 512:1024])[p % 2]
                vector.wait_ge(A_rzs, idx + 1)
                vector.wait_ge(T_at, idx + 1)
                if idx >= 1:
                    vector.wait_ge(A_ath, idx)
                vector.drain()
                vector.tensor_tensor(AFB[:], att_ps, RZB[:, p % 2, :],
                                     ALU.mult).then_inc(V_af, 1)
                vector.wait_ge(A_ath, idx + 1)
                vector.drain()
                vector.tensor_tensor(ATNL[:, p, :], AFB[:], ATNH[:, p, :],
                                     ALU.subtract).then_inc(V_at, 1)

            def _wo_split(s, d):
                oidx = s * DO + d
                g = RAWG[("o", s, d)]
                vector.wait_ge(Sw[0], 16 * (4 + g + 1))
                if d >= 2:
                    vector.wait_ge(T_s, oidx - 1)
                elif s >= 1:
                    vector.wait_ge(T_s, (s - 1) * DO + d % 2 + 7)
                vector.tensor_copy(WOT[:, d % 2, 0, :, :], WR3[:])
                vector.drain()
                vector.tensor_tensor(WOT[:, d % 2, 1, :, :], WR3[:],
                                     WOT[:, d % 2, 0, :, :], ALU.subtract).then_inc(V_wo, 1)

            def _sg(s, d):
                idx = s * DO + d
                s_ps = (ps_tp[:, 0:512], ps_tp[:, 512:1024])[d % 2]
                vector.wait_ge(T_s, idx + 1)
                if d == 0:
                    vector.wait_ge(A_ath, (s + 1) * NP)
                vector.drain()
                vector.tensor_tensor(AFB[:], s_ps, GFM[:, d, SL * s:SL * (s + 1)], ALU.mult)
                vector.drain()
                vector.wait_ge(Sxt[idx % 2], 16 * (idx // 2 + 1))
                if idx >= 2:
                    vector.wait_ge(Sy[idx % 2], 16 * (idx // 2))
                vector.tensor_tensor(YST[:, idx % 2, :], AFB[:], XTC[:, idx % 2, :],
                                     ALU.add).then_inc(V_y, 1)

            for d in range(DO):
                _wq_split(0, d)
                if d >= 2:
                    _q_fin(0, d - 2)
            _q_fin(0, DO - 2)
            _q_fin(0, DO - 1)
            _rz(0)
            for s in range(NS):
                if s + 1 < NS:
                    _wq_split(s + 1, 0)
                    _wq_split(s + 1, 1)
                for p in range(NP):
                    _attn_evac(s, p)
                    if s + 1 < NS:
                        if p + 2 < DO:
                            _wq_split(s + 1, p + 2)
                        if p >= 2:
                            _q_fin(s + 1, p - 2)
                if s + 1 < NS:
                    _q_fin(s + 1, DO - 2)
                    _q_fin(s + 1, DO - 1)
                _wo_split(s, 0)
                _wo_split(s, 1)
                if s + 1 < NS:
                    _rz(s + 1)
                for d in range(2, DO):
                    _wo_split(s, d)
                    _sg(s, d - 2)
                _sg(s, DO - 2)
                _sg(s, DO - 1)

        @block.scalar
        def _(scalar):
            for c in range(NCH + 1):
                cb = c % 2
                if c < NCH:
                    scalar.wait_ge(Sx[c % 2], 16 * (c // 2 + 1))
                    if c >= 2:
                        scalar.wait_ge(T_tp, c - 1)
                    scalar.activation(XNF[:, cb, :], XRAW[:, cb, :],
                                      AF.Square).then_inc(A_sq, 1)
                    scalar.wait_ge(V_st, c + 1)
                    scalar.activation(ST[:, cb, 6:7], ST[:, cb, 5:6], AF.Ln)
                    scalar.drain()
                    scalar.activation(ST[:, cb, 7:8], ST[:, cb, 6:7], AF.Exp,
                                      scale=-0.5).then_inc(A_rs, 1)
                if 1 <= c <= NCH:
                    cl = c - 1
                    scalar.wait_ge(T_tp, cl + 1)
                    scalar.activation(xn_dst(cl, 0), ps_tp3, AF.Copy).then_inc(A_hi, 1)
                    scalar.wait_ge(T_k, cl + 1)
                    if cl >= 1:
                        scalar.wait_ge(V_kt, cl)
                    kp = (ps_k[:, 0:512], ps_k[:, 512:1024])[cl % 2]
                    scalar.activation(KEB[:], kp, AF.Exp)
                    scalar.activation(KRB[:], kp, AF.Relu, bias=1.0).then_inc(A_ke, 1)
            for s in range(NS):
                for h in range(NHO):
                    idx = s * NHO + h
                    gp = (ps_tp[:, 0:512], ps_tp[:, 512:1024])[idx % 2]
                    scalar.wait_ge(T_ga, idx + 1)
                    if idx >= 1:
                        scalar.wait_ge(V_hf, idx)
                    scalar.activation(GSIL[:], gp, AF.Silu, scale=QSCALE).then_inc(A_gs, 1)
                    scalar.wait_ge(V_hf, idx + 1)
                    scalar.activation(HHI[:, h, :], HFB[:], AF.Copy).then_inc(A_hh, 1)
                for d in range(DO):
                    idx = s * DO + d
                    gpp = (ps_kv[:, 512:1024], ps_kv[:, 1024:1536])[idx % 2]
                    scalar.wait_ge(T_dnh, 2 * idx + 2)
                    scalar.activation(GFM[:, d, SL * s:SL * (s + 1)], gpp, AF.Silu,
                                      scale=QSCALE).then_inc(A_g, 1)
            def _qe(s, d):
                idx = s * DO + d
                qp = (ps_k[:, 0:512], ps_k[:, 512:1024],
                      ps_tp[:, 0:512], ps_tp[:, 512:1024])[idx % 4]
                scalar.wait_ge(T_q, idx + 1)
                if s >= 1:
                    scalar.wait_ge(T_at, (s - 1) * DO + d + 1)
                if idx >= 1:
                    scalar.wait_ge(V_q, idx)
                scalar.activation(SCR[:], qp, AF.Exp)
                scalar.activation(QSL[:, d, :], qp, AF.Relu,
                                  bias=1.0).then_inc(A_qe, 1)

            for d in range(DO):
                _qe(0, d)
            for s in range(NS):
                for p in range(NP):
                    idx = s * NP + p
                    scalar.wait_ge(T_rb, idx + 1)
                    if idx >= 2:
                        scalar.wait_ge(V_af, idx - 1)
                    scalar.drain()
                    scalar.activation(RZB[:, p % 2, :], ps_kv[:, 1024:1536],
                                      AF.Copy).then_inc(A_rzs, 1)
                    scalar.wait_ge(V_af, idx + 1)
                    scalar.activation(ATNH[:, p, :], AFB[:], AF.Copy).then_inc(A_ath, 1)
                    if s + 1 < NS and p >= 2:
                        _qe(s + 1, p - 2)
                if s + 1 < NS:
                    _qe(s + 1, DO - 2)
                    _qe(s + 1, DO - 1)
            if debug:
                scalar.wait_ge(Sy[0], 16 * NS * DO // 2)
                scalar.wait_ge(Sy[1], 16 * NS * DO // 2)
                n = 0
                for name, src in [
                    ("dbg_xnh", XNH.rearrange("p a m -> p (a m)")),
                    ("dbg_xnl", XNL.rearrange("p a m -> p (a m)")),
                    ("dbg_atl", ATL.rearrange("p a m -> p (a m)")),
                    ("dbg_kse", KSE.rearrange("p a m -> p (a m)")),
                    ("dbg_g", GFM.rearrange("p a m -> p (a m)")),
                    ("dbg_q", QSL.rearrange("p a m -> p (a m)")),
                    ("dbg_ah", ATNH.rearrange("p a m -> p (a m)")),
                    ("dbg_al", ATNL.rearrange("p a m -> p (a m)")),
                ]:
                    scalar.dma_start(out=dbg[name][:], in_=src).then_inc(Sc, 16)
                    n += 1
                scalar.wait_ge(Sc, 32 + 16 * n)

        @block.tensor
        def _(tensor):
            tensor.wait_ge(Sc, 32)

            def transpose_chunk(c):
                tensor.wait_ge(V_xn, c + 1)
                i = None
                for j in range(8):
                    i = tensor.transpose(ps_tp[:, 128 * j:128 * (j + 1)],
                                         XNF[:, c % 2, 128 * j:128 * (j + 1)], IDT[:])
                i.then_inc(T_tp, 1)

            transpose_chunk(0)

            def kv_mm(c):
                tensor.wait_ge(V_kt, c + 1)
                tensor.wait_ge(V_va, c + 1)
                for p in range(NPL):
                    c0 = kvcol(p)
                    i = tensor.matmul(ps_kv[:, c0:c0 + 129],
                                      KTM[:, c % 2, 128 * p:128 * (p + 1)],
                                      VKV[:, c % 2, p, :], start=(c == 0 and p % 3 == 0),
                                      stop=(c == NCH - 1),
                                      skip_group_check=True)
                i.then_inc(T_kv, 1)

            for c in range(NCH):
                tensor.wait_ge(A_hi, c + 1)
                tensor.wait_ge(V_lo, c + 1)
                if c >= 1:
                    tensor.wait_ge(A_ke, c - 1)
                kp = (ps_k[:, 0:512], ps_k[:, 512:1024])[c % 2]
                first = True
                for d in range(DI):
                    if c == 0:
                        tensor.wait_ge(V_wp, d + 1)
                    for hl, W in ((0, WKH), (0, WKL), (1, WKH)):
                        i = tensor.matmul(kp, xn_tile(c, d, hl), W[:, d, :],
                                          start=first,
                                          stop=(d == DI - 1 and hl == 1),
                                          skip_group_check=True)
                        first = False
                i.then_inc(T_k, 1)
                if c + 1 < NCH:
                    transpose_chunk(c + 1)
                if c >= 1:
                    tensor.wait_ge(V_va, c)
                for d in range(DI):
                    if c == 0:
                        tensor.wait_ge(V_wp, DI + d + 1)
                    for hl, W in ((0, WVH), (0, WVL), (1, WVH)):
                        i = tensor.matmul(ps_v, xn_tile(c, d, hl), W[:, d, :],
                                          start=(d == 0 and hl == 0 and W is WVH),
                                          stop=(d == DI - 1 and hl == 1),
                                          skip_group_check=True)
                i.then_inc(T_v, 1)
                if c >= 1:
                    kv_mm(c - 1)
            kv_mm(NCH - 1)

            for s in range(NS):
                for h in range(NHO):
                    idx = s * NHO + h
                    gp = (ps_tp[:, 0:512], ps_tp[:, 512:1024])[idx % 2]
                    upp = (ps_v, ps_kv[:, 0:512])[idx % 2]
                    tensor.wait_ge(Sgg[idx % 3], 16 * (idx // 3 + 1))
                    if idx >= 2:
                        tensor.wait_ge(A_gs, idx - 1)
                    for d in range(DI):
                        for X in (XNH, XNL):
                            i = tensor.matmul(gp, WGU[:, idx % 3, 0, d, :],
                                              X[:, d, SL * s:SL * (s + 1)],
                                              start=(d == 0 and X is XNH),
                                              stop=(d == DI - 1 and X is XNL),
                                              skip_group_check=True)
                    i.then_inc(T_ga, 1)
                    tensor.wait_ge(Sgu[idx % 3], 16 * (idx // 3 + 1))
                    if idx == 1:
                        tensor.wait_ge(V_stg, 1)
                    if idx >= 2:
                        tensor.wait_ge(V_hf, idx - 1)
                    for d in range(DI):
                        for X in (XNH, XNL):
                            i = tensor.matmul(upp, WGU[:, idx % 3, 1, d, :],
                                              X[:, d, SL * s:SL * (s + 1)],
                                              start=(d == 0 and X is XNH),
                                              stop=(d == DI - 1 and X is XNL),
                                              skip_group_check=True)
                    i.then_inc(T_up, 1)
                tensor.wait_ge(V_hl, s * NHO + NHO)
                for d in range(DO):
                    idx = s * DO + d
                    gpp = (ps_kv[:, 512:1024], ps_kv[:, 1024:1536])[idx % 2]
                    if idx == 0:
                        tensor.wait_ge(V_stg, 1)
                    if idx >= 2:
                        tensor.wait_ge(A_g, idx - 1)
                    for hf in range(2):
                        j = idx * 2 + hf
                        tensor.wait_ge(Sgd[j % 2], 16 * (j // 2 + 1))
                        for h in range(12):
                            hh = 12 * hf + h
                            for HB in (HHI, HLO):
                                i = tensor.matmul(gpp, WDT[:, j % 2, h, :], HB[:, hh, :],
                                                  start=(hf == 0 and h == 0 and HB is HHI),
                                                  stop=(hf == 1 and h == 11 and HB is HLO),
                                                  skip_group_check=True)
                        i.then_inc(T_dnh, 1)

            tensor.wait_ge(A_g, NS * DO)
            tensor.wait_ge(V_asm, 1)
            tensor.wait_ge(A_ke, NCH)
            tensor.wait_ge(V_hf, NS * NHO)
            tensor.wait_ge(A_gs, NS * NHO)

            def _zmm(s, d):
                idx = s * DO + d
                tensor.wait_ge(V_q, idx + 1)
                i = tensor.matmul(ps_v[0:16, :], KSE[:, d, :], QSL[:, d, :],
                                  start=(d == 0), stop=(d == DO - 1),
                                  skip_group_check=True)
                if d == DO - 1:
                    i.then_inc(T_z, 1)

            def _qblock(s, d):
                idx = s * DO + d
                qp = (ps_k[:, 0:512], ps_k[:, 512:1024],
                      ps_tp[:, 0:512], ps_tp[:, 512:1024])[idx % 4]
                tensor.wait_ge(V_wq, idx + 1)
                if idx >= 4:
                    tensor.wait_ge(A_qe, idx - 3)
                if idx % 4 >= 2 and s >= 2:
                    tensor.wait_ge(V_y, (s - 2) * DO + 7 + (idx % 4 - 2))
                i = None
                for dd in range(DI):
                    for hl, src in ((0, 0), (0, 1), (1, 0)):
                        i = tensor.matmul(qp, WQT[:, d % 2, hl, dd, :],
                                          (XNH, XNL)[src][:, dd, SL * s:SL * (s + 1)],
                                          start=(dd == 0 and hl == 0 and src == 0),
                                          stop=(dd == DI - 1 and hl == 1),
                                          skip_group_check=True)
                i.then_inc(T_q, 1)

            def _qphase(s):
                for d in range(DO):
                    _qblock(s, d)
                    if d >= 4:
                        _zmm(s, d - 4)
                for d in range(DO - 4, DO):
                    _zmm(s, d)

            def _pair(s, p):
                idx = s * NP + p
                if idx >= 1:
                    tensor.wait_ge(A_rzs, idx)
                i = tensor.matmul(ps_kv[:, 1024:1536], BPR[:, p, :], RZS[0:16, :],
                                  start=True, stop=True, skip_group_check=True)
                i.then_inc(T_rb, 1)
                att_ps = (ps_kv[:, 0:512], ps_kv[:, 512:1024])[p % 2]
                if idx >= 2:
                    tensor.wait_ge(V_af, idx - 1)
                i = tensor.matmul(att_ps, ATL[:, p, :], QSL[:, p, :],
                                  start=True, stop=True, skip_group_check=True)
                i.then_inc(T_at, 1)

            _qphase(0)
            for s in range(NS):
                tensor.wait_ge(V_rz, s + 1)
                _pair(s, 0)
                _pair(s, 1)
                if s + 1 < NS:
                    for j in range(DO):
                        _qblock(s + 1, j)
                        if j + 2 < NP:
                            _pair(s, j + 2)
                    for d in range(DO):
                        _zmm(s + 1, d)
                else:
                    for p in range(2, NP):
                        _pair(s, p)
                for d in range(DO):
                    idx = s * DO + d
                    sp = (ps_tp[:, 0:512], ps_tp[:, 512:1024])[d % 2]
                    tensor.wait_ge(V_wo, idx + 1)
                    if d == 0 and s + 1 < NS:
                        tensor.wait_ge(A_qe, (s + 2) * DO)
                    if idx >= 2:
                        tensor.wait_ge(V_y, idx - 1)
                    for e in range(DO):
                        if d == 0:
                            tensor.wait_ge(V_at, s * NP + e + 1)
                        for hl, src in ((0, 0), (0, 1), (1, 0)):
                            i = tensor.matmul(sp, WOT[:, d % 2, hl, e, :],
                                              (ATNH, ATNL)[src][:, e, :],
                                              start=(e == 0 and hl == 0 and src == 0),
                                              stop=(e == DO - 1 and hl == 1),
                                              skip_group_check=True)
                    i.then_inc(T_s, 1)

    return nc


_NC_CACHE = {}


def _get_nc(debug):
    if debug not in _NC_CACHE:
        _NC_CACHE[debug] = build_nc(debug)
    return _NC_CACHE[debug]


def kernel(L0_attn_x, L0_attn_Wq, L0_attn_Wk, L0_attn_Wv, L0_attn_Wo,
           L0_ffn_w_gate, L0_ffn_w_up, L0_ffn_w_down, _debug=False):
    x = np.ascontiguousarray(np.asarray(L0_attn_x, dtype=np.float32))
    wq = np.ascontiguousarray(np.asarray(L0_attn_Wq, dtype=np.float32).T)
    wk = np.ascontiguousarray(np.asarray(L0_attn_Wk, dtype=np.float32).T)
    wv = np.ascontiguousarray(np.asarray(L0_attn_Wv, dtype=np.float32).T)
    wo = np.ascontiguousarray(np.asarray(L0_attn_Wo, dtype=np.float32).T)
    wg = np.ascontiguousarray(np.asarray(L0_ffn_w_gate, dtype=np.int32).T)
    wu = np.ascontiguousarray(np.asarray(L0_ffn_w_up, dtype=np.int32).T)
    wd = np.ascontiguousarray(np.asarray(L0_ffn_w_down, dtype=np.int32).T)
    ident = np.eye(128, dtype=np.float32)
    bpair = np.zeros((16, NP * 128), dtype=np.float32)
    for p in range(NP):
        bpair[2 * p, p * 128:p * 128 + 64] = 1.0
        bpair[2 * p + 1, p * 128 + 64:p * 128 + 128] = 1.0

    nc = _get_nc(_debug)
    in_maps = []
    for c in range(8):
        b, h = c // 2, c % 2
        own = x[b, 2048 * h:2048 * (h + 1), :]
        nb = x[b, 2048 * (1 - h):2048 * (2 - h), :]
        x_ord = np.ascontiguousarray(np.concatenate([own, nb], axis=0))
        x_tp = np.ascontiguousarray(own.T)
        wk_c = np.ascontiguousarray(wk[:, 512 * h:512 * (h + 1)])
        wv_c = np.ascontiguousarray(wv[:, 512 * h:512 * (h + 1)])
        in_maps.append({
            "x_ord": x_ord, "x_t": x_tp,
            "wqt": wq, "wkt": wk_c, "wvt": wv_c, "wot": wo,
            "wgt": wg, "wut": wu, "wdt": wd,
            "ident": ident, "bpair": bpair,
        })
    res = run_bass_kernel_spmd(nc, in_maps, core_ids=list(range(8)))
    out = np.empty((4, 4096, D), dtype=np.float32)
    for c in range(8):
        b, h = c // 2, c % 2
        out[b, 2048 * h:2048 * (h + 1), :] = res.results[c]["y"].T
    if _debug:
        return out, res
    return out
